# version 06
# baseline (speedup 1.0000x reference)
import numpy as np
import concourse.bass as bass
import concourse.mybir as mybir
from concourse.bass_utils import run_bass_kernel_spmd

F32 = mybir.dt.float32
BF16 = mybir.dt.bfloat16
AF = mybir.ActivationFunctionType
ALU = mybir.AluOpType
AX = mybir.AxisListType

D = 1024
T = 4096
SC = 512
NSC = T // SC
NDC = 8
NST = 16
NGRP = 4
GN = NST // NGRP
NEXP = 18
NROUT = 16
NET = 4
NEC = 4
NHT = 8
TT2 = 2
LN_EPS = 1e-5

SEM_NAMES = ["dU", "dW", "dWB", "dG", "dS", "dX", "dOut", "dB", "sT", "sV", "sS"]


def build_nc():
    nc = bass.Bass("TRN2", target_bir_lowering=False)

    u_d = nc.dram_tensor("u", [D, T], F32, kind="ExternalInput").ap()
    wx_d = nc.dram_tensor("wx", [128, NDC, 96], F32, kind="ExternalInput").ap()
    wdt_d = nc.dram_tensor("wdt", [64, NDC, 128], F32, kind="ExternalInput").ap()
    bdt_d = nc.dram_tensor("bdt", [128, NDC], F32, kind="ExternalInput").ap()
    alog_d = nc.dram_tensor("alog", [128, NDC * NST], F32, kind="ExternalInput").ap()
    dpar_d = nc.dram_tensor("dpar", [128, NDC], F32, kind="ExternalInput").ap()
    rw_d = nc.dram_tensor("rw", [128, NDC, NST], F32, kind="ExternalInput").ap()
    wexp_d = nc.dram_tensor("wexp", [NEXP, 128, 8192], F32, kind="ExternalInput").ap()
    eye_d = nc.dram_tensor("eye", [128, 128], F32, kind="ExternalInput").ap()

    out_d = nc.dram_tensor("out_fm", [D, T], F32, kind="ExternalOutput").ap()

    x1_scr = nc.dram_tensor("x1_scr", [128, NDC, T], F32, kind="Internal").ap()
    wb_scr = nc.dram_tensor("wb_scr", [NEXP, 128, 8192], BF16, kind="Internal").ap()
    gate_scr = nc.dram_tensor("gate_scr", [NSC, NST, SC], BF16, kind="Internal").ap()
    bc_scr = nc.dram_tensor("bc_scr", [2, 32, SC], BF16, kind="Internal").ap()

    from contextlib import ExitStack
    ctx = ExitStack()
    e = ctx.enter_context

    u_sb = e(nc.sbuf_tensor("u_sb", [128, NDC, SC], F32))
    u_b = e(nc.sbuf_tensor("u_b", [128, 2, SC], BF16))
    dt_sb = e(nc.sbuf_tensor("dt_sb", [128, NDC, SC], F32))
    dtu_sb = e(nc.sbuf_tensor("dtu_sb", [128, SC], BF16))
    bbc_sb = e(nc.sbuf_tensor("bbc_sb", [128, NST, SC], BF16))
    cbc_sb = e(nc.sbuf_tensor("cbc_sb", [128, NST, SC], BF16))
    b_in = e(nc.sbuf_tensor("b_in", [128, GN, SC], BF16))
    h8 = e(nc.sbuf_tensor("h8", [128, GN, SC], F32))
    abar = e(nc.sbuf_tensor("abar", [128, 4, SC], F32))
    ym_sb = e(nc.sbuf_tensor("ym_sb", [128, SC], F32))
    state_sb = e(nc.sbuf_tensor("state_sb", [128, NDC * NST], F32))
    dtraw_b = e(nc.sbuf_tensor("dtraw_b", [64, SC], BF16))
    spa_sb = e(nc.sbuf_tensor("spa_sb", [128, SC], F32))
    bcsel_sb = e(nc.sbuf_tensor("bcsel_sb", [32, SC], BF16))
    x1b_sb = e(nc.sbuf_tensor("x1b_sb", [128, 2, NDC, SC], BF16))
    yacc_sb = e(nc.sbuf_tensor("yacc_sb", [128, NHT, SC], F32))
    wslot_sb = e(nc.sbuf_tensor("wslot_sb", [128, 3, 8192], BF16))
    act_sb = e(nc.sbuf_tensor("act_sb", [128, 2, NET, SC], BF16))
    gbc_sb = e(nc.sbuf_tensor("gbc_sb", [128, 2, SC], BF16))
    pb_sb = e(nc.sbuf_tensor("pb_sb", [128, SC], BF16))
    sqb_sb = e(nc.sbuf_tensor("sqb_sb", [128, SC], BF16))
    srow = e(nc.sbuf_tensor("srow", [1, SC], F32))
    sqrow = e(nc.sbuf_tensor("sqrow", [1, SC], F32))
    vrow = e(nc.sbuf_tensor("vrow", [1, SC], F32))
    lg_sb = e(nc.sbuf_tensor("lg_sb", [128, 4, NST], F32))
    wk_sb = e(nc.sbuf_tensor("wk_sb", [128, 4, NST], F32))
    msk_sb = e(nc.sbuf_tensor("msk_sb", [128, 4, NST], F32))
    mx_sb = e(nc.sbuf_tensor("mx_sb", [128, 4], F32))
    ex_sb = e(nc.sbuf_tensor("ex_sb", [128, 4, NST], F32))
    asum_sb = e(nc.sbuf_tensor("asum_sb", [128, 4], F32))
    rs_sb = e(nc.sbuf_tensor("rs_sb", [128, 4], F32))
    gt_sb = e(nc.sbuf_tensor("gt_sb", [128, 4, NST], F32))
    gfm_sb = e(nc.sbuf_tensor("gfm_sb", [16, SC], BF16))
    wx_b = e(nc.sbuf_tensor("wx_b", [128, NDC, 96], BF16))
    wdt_b = e(nc.sbuf_tensor("wdt_b", [64, NDC, 128], BF16))
    bdt_s = e(nc.sbuf_tensor("bdt_s", [128, NDC], F32))
    dp1_s = e(nc.sbuf_tensor("dp1_s", [128, NDC], F32))
    a_s = e(nc.sbuf_tensor("a_s", [128, NDC * NST], F32))
    rw_s = e(nc.sbuf_tensor("rw_s", [128, NDC, NST], F32))
    eye_s = e(nc.sbuf_tensor("eye_s", [128, 128], F32))
    ones_b = e(nc.sbuf_tensor("ones_b", [128, 1], BF16))
    ones1f = e(nc.sbuf_tensor("ones1f", [1, 128], F32))

    ps_big = e(nc.psum_tensor("ps_big", [128, SC], F32))
    ps_up = e(nc.psum_tensor("ps_up", [128, 3, SC], F32))
    ps_y = e(nc.psum_tensor("ps_y", [128, NHT, SC // 2], F32))

    sems = {n: e(nc.semaphore(n)) for n in SEM_NAMES}

    ops = []
    C = {n: 0 for n in SEM_NAMES}

    def emit(eng, fn, waits=(), inc=None):
        ops.append((eng, fn, tuple(w for w in waits if w[1] > 0), inc))
        if inc is not None:
            C[inc[0]] += inc[1]

    def cnt(s):
        return C[s]

    def dma(eng, sem, out, in_, waits=(), accum=False):
        if accum:
            emit(eng, lambda E, o=out, i=in_: E.dma_start(out=o, in_=i, accum_op=ALU.add),
                 waits, (sem, 16))
        else:
            emit(eng, lambda E, o=out, i=in_: E.dma_start(out=o, in_=i), waits, (sem, 16))

    def mm(out, lhsT, rhs, start, stop, waits=(), inc=0):
        emit("tensor",
             lambda E, o=out, l=lhsT, r=rhs, st=start, sp=stop:
             E.matmul(out=o, lhsT=l, rhs=r, start=st, stop=sp),
             waits, (("sT", inc) if inc else None))

    def vop(fn, waits=(), inc=0):
        emit("vector", fn, waits, (("sV", inc) if inc else None))

    def sop(fn, waits=(), inc=0):
        emit("scalar", fn, waits, (("sS", inc) if inc else None))

    def vsync():
        emit("vector", lambda E: None, [("sV", cnt("sV"))], None)

    def ssync():
        emit("scalar", lambda E: None, [("sS", cnt("sS"))], None)

    dma("sync", "dS", bdt_s[:], bdt_d[:])
    dma("sync", "dS", dp1_s[:], dpar_d[:])
    dma("sync", "dS", a_s[:], alog_d[:])
    dma("sync", "dS", rw_s[:], rw_d[:])
    dma("sync", "dS", eye_s[:], eye_d[:])
    dma("gpsimd", "dB", wx_b[:], wx_d[:])
    dma("gpsimd", "dB", wdt_b[:], wdt_d[:])
    init_dS = cnt("dS")
    init_dB = cnt("dB")
    vop(lambda E: E.memset(ones_b[:], 1.0))
    vop(lambda E: E.memset(ones1f[:], 1.0))
    sop(lambda E: E.activation(out=a_s[:], in_=a_s[:], func=AF.Exp),
        waits=[("dS", init_dS)], inc=1)
    vop(lambda E: E.tensor_scalar_mul(out=a_s[:], in0=a_s[:], scalar1=-1.0),
        waits=[("sS", cnt("sS"))], inc=1)
    v_a_neg = cnt("sV")
    vop(lambda E: E.tensor_scalar_add(out=dp1_s[:], in0=dp1_s[:], scalar1=1.0),
        waits=[("dS", init_dS)], inc=1)
    v_dp1 = cnt("sV")

    last = {
        "u_free": [],
        "ub_free": [("sT", 0), ("sT", 0)],
        "psbig_free": ("sS", 0),
        "bc_free": ("sV", 0),
        "x1b_free": [("sT", 0), ("sT", 0)],
        "gbc_free": [("sV", 0)] * 2,
        "wslot_free": [("sT", 0)] * 3,
        "wslot_wb": [("dWB", 0)] * 3,
        "act_free": ("sT", 0),
        "yacc_free": ("dOut", 0),
        "abar_free": [("sV", 0)] * 4,
        "psy_free": ("sV", 0),
        "psup_free": [("sV", 0), ("sV", 0), ("sV", 0)],
        "bcscr_free": [("dB", 0), ("dB", 0)],
        "bcsel_free": ("dS", 0),
        "gt_free": ("sT", 0),
        "gfm_free": ("dS", 0),
    }

    def layer_norm(plane, n_sum_ap, n_sq_ap, bc_ap, ready_waits, sum_war, sq_war, bc_war,
                   sq_start=True):
        t_stats = 0
        for dc in range(NDC):
            w = list(ready_waits(dc))
            if t_stats:
                w.append(("sT", t_stats))
            sop(lambda E, dc=dc: E.activation(out=pb_sb[:], in_=plane[:, dc, :], func=AF.Copy),
                waits=w, inc=1)
            sop(lambda E, dc=dc: E.activation(out=sqb_sb[:], in_=plane[:, dc, :],
                                              func=AF.Square), inc=1)
            s_cast = cnt("sS")
            w0 = [("sS", s_cast)]
            if dc == 0:
                w0 += sum_war
            mm(n_sum_ap, ones_b[:], pb_sb[:], start=(dc == 0), stop=(dc == NDC - 1), waits=w0)
            w1 = []
            if dc == 0:
                w1 += sq_war
            mm(n_sq_ap, ones_b[:], sqb_sb[:], start=(dc == 0 and sq_start),
               stop=(dc == NDC - 1), waits=w1, inc=1)
            t_stats = cnt("sT")
        sop(lambda E: E.activation(out=srow[:], in_=n_sum_ap, func=AF.Copy,
                                   scale=1.0 / D),
            waits=[("sT", t_stats)], inc=1)
        sop(lambda E: E.activation(out=sqrow[:], in_=n_sq_ap, func=AF.Copy,
                                   scale=1.0 / D), inc=1)
        s_sums = cnt("sS")
        vop(lambda E: E.tensor_mul(out=vrow[:], in0=srow[:], in1=srow[:]),
            waits=[("sS", s_sums)], inc=1)
        vsync()
        vop(lambda E: E.tensor_sub(out=vrow[:], in0=sqrow[:], in1=vrow[:]), inc=1)
        vsync()
        vop(lambda E: E.tensor_scalar_add(out=vrow[:], in0=vrow[:], scalar1=LN_EPS), inc=1)
        v_var = cnt("sV")
        sop(lambda E: E.activation(out=vrow[:], in_=vrow[:], func=AF.Ln),
            waits=[("sV", v_var)], inc=1)
        ssync()
        sop(lambda E: E.activation(out=vrow[:], in_=vrow[:], func=AF.Exp, scale=-0.5), inc=1)
        s_inv = cnt("sS")
        mm(bc_ap, ones1f[:], srow[:], start=True, stop=True,
           waits=[("sV", v_var)] + bc_war, inc=1)
        t_mbc = cnt("sT")
        for dc in range(NDC):
            vop(lambda E, dc=dc: E.tensor_sub(out=plane[:, dc, :], in0=plane[:, dc, :],
                                              in1=bc_ap),
                waits=([("sT", t_mbc)] if dc == 0 else ()), inc=(1 if dc == NDC - 1 else 0))
        v_sub = cnt("sV")
        mm(bc_ap, ones1f[:], vrow[:], start=True, stop=True,
           waits=[("sS", s_inv), ("sV", v_sub)], inc=1)
        t_ibc = cnt("sT")
        for dc in range(NDC):
            vop(lambda E, dc=dc: E.tensor_mul(out=plane[:, dc, :], in0=plane[:, dc, :],
                                              in1=bc_ap),
                waits=([("sT", t_ibc)] if dc == 0 else ()), inc=(1 if dc == NDC - 1 else 0))
        return cnt("sV"), s_sums

    GATE_ON_GP = True
    A = [dict() for _ in range(NSC)]

    def issue_uload(c):
        dma("scalar", "dU", u_sb[:],
            u_d.rearrange("(dc p) t -> p dc t", p=128)[:, :, c * SC:(c + 1) * SC],
            waits=last["u_free"])
        A[c]["dU"] = cnt("dU")

    def a_head(c):
        st = A[c]
        dU_c = st["dU"]
        t_xdbl = 0
        for dc in range(NDC):
            w = [("dU", dU_c)]
            w.append(last["ub_free"][dc % 2])
            sop(lambda E, dc=dc: E.activation(out=u_b[:, dc % 2, :], in_=u_sb[:, dc, :],
                                              func=AF.Copy),
                waits=w, inc=1)
            s_cast_dc = cnt("sS")
            w = [("sS", s_cast_dc), ("dB", init_dB)]
            if dc == 0:
                w.append(last["psbig_free"])
            mm(ps_big[0:96, :], wx_b[:, dc, :], u_b[:, dc % 2, :],
               start=(dc == 0), stop=(dc == NDC - 1), waits=w, inc=1)
            t_xdbl = cnt("sT")
            last["ub_free"][dc % 2] = ("sT", t_xdbl)
        st["t_xdbl"] = t_xdbl
        sop(lambda E: E.activation(out=dtraw_b[:], in_=ps_big[0:64, :], func=AF.Copy),
            waits=[("sT", t_xdbl)], inc=1)
        sop(lambda E: E.activation(out=bcsel_sb[:], in_=ps_big[64:96, :], func=AF.Copy),
            waits=[last["bcsel_free"]], inc=1)
        s_bc = cnt("sS")
        st["s_bc"] = s_bc
        last["psbig_free"] = ("sS", s_bc)
        dma("sync", "dS", bc_scr[c % 2, :, :], bcsel_sb[:],
            waits=[("sS", s_bc), last["bcscr_free"][c % 2]])
        dS_bc = cnt("dS")
        last["bcsel_free"] = ("dS", dS_bc)
        dma("sync", "dB", bbc_sb[:],
            bc_scr[c % 2, 0:16, :].partition_broadcast(128),
            waits=[("dS", dS_bc), last["bc_free"]])
        dma("sync", "dB", cbc_sb[:],
            bc_scr[c % 2, 16:32, :].partition_broadcast(128))
        st["dB_bc"] = cnt("dB")
        last["bcscr_free"][c % 2] = ("dB", st["dB_bc"])
        st["v_dt"] = [0] * NDC
        st["s_dtps"] = [0] * NDC
        st["v_prefold"] = [0] * NDC

    filler = []

    def pop_filler():
        if filler:
            filler.pop(0)()

    def flush_filler():
        while filler:
            filler.pop(0)()

    def push_a_dc(c, dc):
        st = A[c]

        def unit_dt():
            v_dt = st["v_dt"]
            s_dtps = st["s_dtps"]
            w = [("sS", st["s_bc"])]
            if dc == 0:
                w.append(last["psbig_free"])
            else:
                w.append(("sS", s_dtps[dc - 1]))
            mm(ps_big[:], wdt_b[:, dc, :], dtraw_b[:], start=True, stop=True, waits=w, inc=1)
            t_here = cnt("sT")
            sop(lambda E, dc=dc: E.activation(out=spa_sb[:], in_=ps_big[:],
                                              func=AF.Abs, bias=bdt_s[:, dc:dc + 1]),
                waits=[("sT", t_here), ("sV", 0 if dc == 0 else v_dt[dc - 1])], inc=1)
            sop(lambda E, dc=dc: E.activation(out=dt_sb[:, dc, :], in_=ps_big[:],
                                              func=AF.Relu, bias=bdt_s[:, dc:dc + 1]), inc=1)
            s_dtps[dc] = cnt("sS")
            sop(lambda E: E.activation(out=spa_sb[:], in_=spa_sb[:], func=AF.Exp,
                                       scale=-1.0), inc=1)
            sop(lambda E: E.activation(out=spa_sb[:], in_=spa_sb[:], func=AF.Ln,
                                       bias=1.0), inc=1)
            s_spl = cnt("sS")
            vop(lambda E, dc=dc: E.tensor_add(out=dt_sb[:, dc, :], in0=dt_sb[:, dc, :],
                                              in1=spa_sb[:]),
                waits=[("sS", s_spl)], inc=1)
            v_dt[dc] = cnt("sV")
            vop(lambda E, dc=dc: E.tensor_mul(out=dtu_sb[:], in0=u_sb[:, dc, :],
                                              in1=dt_sb[:, dc, :]),
                waits=[("dU", st["dU"]), ("sV", v_dp1)], inc=1)

        def unit_grp(grp):
            v_dt = st["v_dt"]
            n0 = grp * GN
            dtu_bc = dtu_sb[:].unsqueeze(1).to_broadcast((128, GN, SC))
            vop(lambda E, dbc=dtu_bc, g=grp: E.tensor_tensor(
                    out=b_in[:], in0=dbc, in1=bbc_sb[:, g * GN:(g + 1) * GN, :],
                    op=ALU.mult),
                waits=[("dB", st["dB_bc"])], inc=1)
            for j in range(GN):
                n = n0 + j
                col = dc * NST + n
                sop(lambda E, dc=dc, col=col: E.activation(
                        out=abar[:, col % 4, :], in_=dt_sb[:, dc, :], func=AF.Exp,
                        scale=a_s[:, col:col + 1]),
                    waits=[("sV", v_dt[dc]),
                           last["abar_free"][col % 4]], inc=1)
                s_exp = cnt("sS")
                init = 0.0 if c == 0 else state_sb[:, col:col + 1]
                vop(lambda E, j=j, col=col, init=init: E.tensor_tensor_scan(
                        out=h8[:, j, :], data0=abar[:, col % 4, :], data1=b_in[:, j, :],
                        initial=init, op0=ALU.mult, op1=ALU.add),
                    waits=[("sS", s_exp)], inc=1)
                last["abar_free"][col % 4] = ("sV", cnt("sV"))
            vsync()
            vop(lambda E, dc=dc, n0=n0: E.tensor_copy(
                    out=state_sb[:, dc * NST + n0: dc * NST + n0 + GN],
                    in_=h8[:, :, SC - 1]), inc=1)
            vop(lambda E, g=grp: E.tensor_tensor(
                    out=h8[:].rearrange("p a b -> p (a b)"),
                    in0=h8[:].rearrange("p a b -> p (a b)"),
                    in1=cbc_sb[:, g * GN:(g + 1) * GN, :].rearrange("p a b -> p (a b)"),
                    op=ALU.mult), inc=1)
            vop(lambda E: E.tensor_add(out=h8[:, 0:2, :].rearrange("p a b -> p (a b)"),
                                       in0=h8[:, 0:2, :].rearrange("p a b -> p (a b)"),
                                       in1=h8[:, 2:4, :].rearrange("p a b -> p (a b)")))
            if grp == 0:
                vop(lambda E: E.tensor_add(out=ym_sb[:], in0=h8[:, 0, :], in1=h8[:, 1, :]),
                    inc=1)
            else:
                vop(lambda E: E.tensor_add(out=h8[:, 0, :], in0=h8[:, 0, :], in1=h8[:, 1, :]))
                vop(lambda E: E.tensor_add(out=ym_sb[:], in0=ym_sb[:], in1=h8[:, 0, :]),
                    inc=1)
            if grp == NGRP - 1:
                vop(lambda E, dc=dc: E.scalar_tensor_tensor(
                        out=u_sb[:, dc, :], in0=u_sb[:, dc, :], scalar=dp1_s[:, dc:dc + 1],
                        in1=ym_sb[:], op0=ALU.mult, op1=ALU.add), inc=1)
                st["v_prefold"][dc] = cnt("sV")
                if dc == NDC - 1:
                    last["bc_free"] = ("sV", st["v_prefold"][NDC - 1])

        filler.append(unit_dt)
        for grp in range(NGRP):
            filler.append(lambda g=grp: unit_grp(g))

    def a_tail(c):
        st = A[c]
        v_prefold = st["v_prefold"]
        v_x1, s_sums1 = layer_norm(
            u_sb, ps_big[0:1, :], ps_big[32:33, :], ps_big[:],
            ready_waits=lambda dc: [("sV", v_prefold[dc])],
            sum_war=[("sS", st["s_dtps"][NDC - 1])],
            sq_war=[],
            bc_war=[],
            sq_start=False,
        )
        sop(lambda E, c=c: E.activation(
                out=x1b_sb[:, c % 2, :, :].rearrange("p a b -> p (a b)"),
                in_=u_sb[:].rearrange("p a b -> p (a b)"), func=AF.Copy),
            waits=[("sV", v_x1), last["x1b_free"][c % 2]], inc=1)
        st["s_x1b"] = cnt("sS")
        dma("sync", "dS", x1_scr[:, :, c * SC:(c + 1) * SC], u_sb[:], waits=[("sV", v_x1)])
        st["dS_x1"] = cnt("dS")

        for tt in range(4):
            for dc in range(NDC):
                w = [("sV", v_x1)]
                mm(ps_big[:, tt * NST:(tt + 1) * NST],
                   u_sb[:, dc, tt * 128:(tt + 1) * 128], rw_s[:, dc, :],
                   start=(tt == 0 and dc == 0), stop=(tt == 3 and dc == NDC - 1),
                   waits=w, inc=(1 if tt == 3 and dc == NDC - 1 else 0))
        t_lg = cnt("sT")
        sop(lambda E: E.activation(out=lg_sb[:].rearrange("p a b -> p (a b)"),
                                   in_=ps_big[:, 0:64], func=AF.Copy),
            waits=[("sT", t_lg)], inc=1)
        s_lg = cnt("sS")
        vop(lambda E: E.tensor_copy(out=wk_sb[:].rearrange("p a b -> p (a b)"),
                                    in_=lg_sb[:].rearrange("p a b -> p (a b)")),
            waits=[("sS", s_lg), last["gt_free"]], inc=1)
        for it in range(4):
            vsync()
            vop(lambda E: E.tensor_reduce(out=mx_sb[:], in_=wk_sb[:], axis=AX.X, op=ALU.max),
                inc=1)
            if it < 3:
                vsync()
                mx_bc = mx_sb[:].unsqueeze(2).to_broadcast((128, 4, NST))
                vop(lambda E, m=mx_bc: E.tensor_tensor(out=msk_sb[:], in0=wk_sb[:], in1=m,
                                                       op=ALU.is_ge), inc=1)
                vsync()
                vop(lambda E: E.scalar_tensor_tensor(
                        out=wk_sb[:].rearrange("p a b -> p (a b)"),
                        in0=msk_sb[:].rearrange("p a b -> p (a b)"), scalar=-1e30,
                        in1=wk_sb[:].rearrange("p a b -> p (a b)"),
                        op0=ALU.mult, op1=ALU.add), inc=1)
        vsync()
        mx_bc = mx_sb[:].unsqueeze(2).to_broadcast((128, 4, NST))
        vop(lambda E, m=mx_bc: E.tensor_tensor(out=msk_sb[:], in0=lg_sb[:], in1=m,
                                               op=ALU.is_ge), inc=1)
        vop(lambda E: E.tensor_reduce(out=mx_sb[:], in_=lg_sb[:], axis=AX.X, op=ALU.max,
                                      negate=True), inc=1)
        v_topk = cnt("sV")
        for tt in range(4):
            sop(lambda E, tt=tt: E.activation(out=ex_sb[:, tt, :], in_=lg_sb[:, tt, :],
                                              func=AF.Exp, bias=mx_sb[:, tt:tt + 1],
                                              accum_out=asum_sb[:, tt:tt + 1]),
                waits=([("sV", v_topk)] if tt == 0 else ()), inc=(1 if tt == 3 else 0))
        s_ex = cnt("sS")
        vop(lambda E: E.reciprocal(out=rs_sb[:], in_=asum_sb[:]),
            waits=[("sS", s_ex)], inc=1)
        vsync()
        rs_bc = rs_sb[:].unsqueeze(2).to_broadcast((128, 4, NST))
        vop(lambda E, rb=rs_bc: E.tensor_tensor(out=gt_sb[:], in0=ex_sb[:], in1=rb,
                                                op=ALU.mult), inc=1)
        vsync()
        vop(lambda E: E.tensor_mul(out=gt_sb[:].rearrange("p a b -> p (a b)"),
                                   in0=gt_sb[:].rearrange("p a b -> p (a b)"),
                                   in1=msk_sb[:].rearrange("p a b -> p (a b)")), inc=1)
        v_gate = cnt("sV")
        s_gfm = cnt("sS")
        for tt in range(4):
            w = [("sV", v_gate), ("dS", init_dS)]
            if tt == 0:
                w.append(("sS", s_lg))
            else:
                w.append(("sS", s_gfm))
            emit("tensor",
                 lambda E, tt=tt: E.transpose(out=ps_big[0:16, tt * 128:(tt + 1) * 128],
                                              in_=gt_sb[:, tt, :], identity=eye_s[:]),
                 w, ("sT", 1))
            t_tr = cnt("sT")
            sop(lambda E, tt=tt: E.activation(out=gfm_sb[:, tt * 128:(tt + 1) * 128],
                                              in_=ps_big[0:16, tt * 128:(tt + 1) * 128],
                                              func=AF.Copy),
                waits=[("sT", t_tr)] + ([last["gfm_free"]] if tt == 0 else []), inc=1)
            s_gfm = cnt("sS")
        last["psbig_free"] = ("sS", s_gfm)
        last["gt_free"] = ("sT", cnt("sT"))
        dma("sync", "dS", gate_scr[c, :, :], gfm_sb[:], waits=[("sS", s_gfm)])
        st["dS_gate"] = cnt("dS")
        last["gfm_free"] = ("dS", st["dS_gate"])
        last["u_free"] = [("sV", v_x1), ("sS", st["s_x1b"]), ("dS", st["dS_x1"]),
                          ("sT", t_lg)]

    def load_expert(c, st, r):
        slot = r % 3
        w = [last["wslot_free"][slot]]
        if c == 0:
            w.append(last["wslot_wb"][slot])
            dma("gpsimd", "dW", wslot_sb[:, slot, :], wexp_d[r], waits=w)
            st["dW_after"][r] = cnt("dW")
            dma("scalar", "dWB", wb_scr[r], wslot_sb[:, slot, :],
                waits=[("dW", st["dW_after"][r])])
            last["wslot_wb"][slot] = ("dWB", cnt("dWB"))
        else:
            dma("scalar", "dW", wslot_sb[:, slot, :], wb_scr[r],
                waits=w + [("dWB", last["wb_done"])])
            st["dW_after"][r] = cnt("dW")

    def load_gate(c, st, r):
        dma("sync", "dG", gbc_sb[:, r % 2, :],
            gate_scr[c, r, :].partition_broadcast(128),
            waits=[("dS", A[c]["dS_gate"]), last["gbc_free"][r % 2]])
        st["dG_after"][r] = cnt("dG")

    B = [dict() for _ in range(NSC)]

    def b_head(c):
        st = B[c]
        st["dW_after"] = {}
        st["dG_after"] = {}
        st["upslot"] = 0
        for r in range(3):
            load_expert(c, st, r)
        for r in range(2):
            load_gate(c, st, r)
        if c + 1 < NSC:
            issue_uload(c + 1)

    def b_group(c, g):
        st = B[c]
        rr = [2 * g, 2 * g + 1] if g < 8 else [16, 17]
        s_gelu = {}
        v_gmul = {}
        g_eng = "gpsimd" if GATE_ON_GP else "vector"
        g_sem = "dB" if GATE_ON_GP else "sV"

        def ups_for(ri, r):
            for et in range(NET):
                for dc in range(NDC):
                    w = []
                    if dc == 0:
                        w = [("dW", st["dW_after"][r]), ("sS", A[c]["s_x1b"]),
                             last["psup_free"][st["upslot"]]]
                    mm(ps_up[:, st["upslot"], :],
                       wslot_sb[:, r % 3, dc * 512 + et * 128: dc * 512 + (et + 1) * 128],
                       x1b_sb[:, c % 2, dc, :],
                       start=(dc == 0), stop=(dc == NDC - 1), waits=w,
                       inc=(1 if dc == NDC - 1 else 0))
                t_up = cnt("sT")
                sop(lambda E, ri=ri, et=et, sl=st["upslot"]: E.activation(
                        out=act_sb[:, ri, et, :], in_=ps_up[:, sl, :], func=AF.Gelu),
                    waits=[("sT", t_up), last["act_free"]], inc=1)
                s_gelu[(ri, et)] = cnt("sS")
                last["psup_free"][st["upslot"]] = ("sS", cnt("sS"))
                st["upslot"] = (st["upslot"] + 1) % 3
            pop_filler()
            if r < NROUT:
                for ec in range(NEC):
                    emit(g_eng,
                         lambda E, ri=ri, ec=ec, r=r: E.tensor_tensor(
                             out=act_sb[:, ri, ec, :], in0=act_sb[:, ri, ec, :],
                             in1=gbc_sb[:, r % 2, :],
                             op=ALU.mult),
                         [("sS", s_gelu[(ri, ec)]), ("dG", st["dG_after"][r])],
                         (g_sem, 1))
                v_gmul[ri] = (g_sem, cnt(g_sem))
                last["gbc_free"][r % 2] = (g_sem, cnt(g_sem))
                if r + 2 < NROUT:
                    load_gate(c, st, r + 2)
            else:
                v_gmul[ri] = None

        def downs_for(ri, r, tt2, start_sess, stop_sess):
            for ec in range(NEC):
                for ht in range(NHT):
                    w = []
                    if ec == 0 and ht == 0:
                        if v_gmul[ri] is not None:
                            w.append(v_gmul[ri])
                        else:
                            w.append(("sS", s_gelu[(ri, NET - 1)]))
                        if start_sess:
                            w.append(last["psy_free"])
                    mm(ps_y[:, ht, :],
                       wslot_sb[:, r % 3,
                                4096 + ec * 1024 + ht * 128:4096 + ec * 1024 + (ht + 1) * 128],
                       act_sb[:, ri, ec, tt2 * 256:(tt2 + 1) * 256],
                       start=(start_sess and ec == 0 and ht % 2 == 0),
                       stop=(stop_sess and ec == NEC - 1),
                       waits=w,
                       inc=(1 if stop_sess and ec == NEC - 1 and ht == NHT - 1 else 0))

        def evacs(tt2):
            t_down = cnt("sT")
            for hh in range(2):
                psv = ps_y[:, hh * 4:(hh + 1) * 4, :]
                ysl = yacc_sb[:, hh * 4:(hh + 1) * 4, tt2 * 256:(tt2 + 1) * 256]
                w = [("sT", t_down)]
                if g == 0 and hh == 0:
                    w.append(last["yacc_free"])
                if g == 0:
                    vop(lambda E, pv=psv, ys=ysl: E.tensor_copy(out=ys, in_=pv),
                        waits=w, inc=1)
                else:
                    vop(lambda E, pv=psv, ys=ysl: E.tensor_add(out=ys, in0=ys, in1=pv),
                        waits=w, inc=1)
                st["v_evac"][(tt2, hh)] = cnt("sV")
            last["psy_free"] = ("sV", cnt("sV"))

        st.setdefault("v_evac", {})
        ups_for(0, rr[0])
        downs_for(0, rr[0], 0, start_sess=True, stop_sess=False)
        ups_for(1, rr[1])
        downs_for(1, rr[1], 0, start_sess=False, stop_sess=True)
        evacs(0)
        pop_filler()
        downs_for(0, rr[0], 1, start_sess=True, stop_sess=False)
        downs_for(1, rr[1], 1, start_sess=False, stop_sess=True)
        evacs(1)
        pop_filler()
        t_g_end = cnt("sT")
        last["act_free"] = ("sT", t_g_end)
        for r in rr:
            last["wslot_free"][r % 3] = ("sT", t_g_end)
            if r + 3 < NEXP:
                load_expert(c, st, r + 3)

    def b_tail(c):
        st = B[c]
        if c == 0:
            last["wb_done"] = cnt("dWB")
        last["x1b_free"][c % 2] = ("sT", cnt("sT"))
        dX_hh = [0, 0]
        for hh in range(2):
            dma("gpsimd", "dX", yacc_sb[:, hh * 4:(hh + 1) * 4, :],
                x1_scr[:, hh * 4:(hh + 1) * 4, c * SC:(c + 1) * SC],
                waits=[("sV", st["v_evac"][(1, hh)]), ("dS", A[c]["dS_x1"])], accum=True)
            dX_hh[hh] = cnt("dX")
        v_x2, _ = layer_norm(
            yacc_sb, ps_up[0:1, 0, :], ps_up[32:33, 0, :], ps_up[:, 1, :],
            ready_waits=lambda dc: [("dX", dX_hh[dc // 4])],
            sum_war=[last["psup_free"][0]],
            sq_war=[],
            bc_war=[last["psup_free"][1]],
            sq_start=False,
        )
        last["psup_free"] = [("sV", v_x2), ("sV", v_x2), ("sV", v_x2)]
        dma("sync", "dOut",
            out_d.rearrange("(dc p) t -> p dc t", p=128)[:, :, c * SC:(c + 1) * SC],
            yacc_sb[:], waits=[("sV", v_x2)])
        last["yacc_free"] = ("dOut", cnt("dOut"))

    issue_uload(0)
    a_head(0)
    for dc in range(NDC):
        push_a_dc(0, dc)
    flush_filler()
    a_tail(0)
    for c in range(NSC):
        b_head(c)
        b_group(c, 0)
        if c + 1 < NSC:
            a_head(c + 1)
            for dc in range(NDC):
                push_a_dc(c + 1, dc)
        for g in range(1, 9):
            b_group(c, g)
        flush_filler()
        b_tail(c)
        if c + 1 < NSC:
            a_tail(c + 1)

    emit("sync", lambda E: None, [("dOut", cnt("dOut"))], None)

    with nc.Block() as block:
        def replay(name):
            def f(E):
                for eng, fn, waits, inc in ops:
                    if eng != name:
                        continue
                    for s, v in waits:
                        E.wait_ge(sems[s], v)
                    ins = fn(E)
                    if inc is not None and ins is not None:
                        ins.then_inc(sems[inc[0]], inc[1])
            return f

        block.sync(replay("sync"))
        block.gpsimd(replay("gpsimd"))
        block.tensor(replay("tensor"))
        block.vector(replay("vector"))
        block.scalar(replay("scalar"))

    ctx.close()
    return nc


_NC_CACHE = None


def _get_nc():
    global _NC_CACHE
    if _NC_CACHE is None:
        _NC_CACHE = build_nc()
    return _NC_CACHE


def kernel(L0_attn_u, L0_attn_A_log, L0_attn_W_x, L0_attn_W_dt, L0_attn_b_dt,
           L0_attn_D_param, L0_ffn_shared_up, L0_ffn_shared_down, L0_ffn_routed_up,
           L0_ffn_routed_down, L0_ffn_router_w):
    u = np.asarray(L0_attn_u, dtype=np.float32)
    W_x = np.asarray(L0_attn_W_x, dtype=np.float32)
    W_dt = np.asarray(L0_attn_W_dt, dtype=np.float32)
    b_dt = np.asarray(L0_attn_b_dt, dtype=np.float32)
    A_log = np.asarray(L0_attn_A_log, dtype=np.float32)
    D_par = np.asarray(L0_attn_D_param, dtype=np.float32)
    sh_up = np.asarray(L0_ffn_shared_up, dtype=np.float32)
    sh_dn = np.asarray(L0_ffn_shared_down, dtype=np.float32)
    ro_up = np.asarray(L0_ffn_routed_up, dtype=np.float32)
    ro_dn = np.asarray(L0_ffn_routed_down, dtype=np.float32)
    r_w = np.asarray(L0_ffn_router_w, dtype=np.float32)

    wx_h = np.ascontiguousarray(W_x.T.reshape(8, 128, 96).transpose(1, 0, 2))
    wdt_h = np.ascontiguousarray(W_dt.reshape(8, 128, 64).transpose(2, 0, 1))
    bdt_h = np.ascontiguousarray(b_dt.reshape(8, 128).T)
    alog_h = np.ascontiguousarray(
        A_log.reshape(8, 128, 16).transpose(1, 0, 2).reshape(128, 128))
    dpar_h = np.ascontiguousarray(D_par.reshape(8, 128).T)
    rw_h = np.ascontiguousarray(r_w.T.reshape(8, 128, 16).transpose(1, 0, 2))
    eye_h = np.eye(128, dtype=np.float32)

    ups = np.concatenate([ro_up, sh_up], axis=0)
    dns = np.concatenate([ro_dn, sh_dn], axis=0)
    up_sb = ups.transpose(0, 2, 1).reshape(18, 8, 128, 512).transpose(0, 2, 1, 3) \
        .reshape(18, 128, 4096)
    dn_sb = dns.transpose(0, 2, 1).reshape(18, 4, 128, 1024).transpose(0, 2, 1, 3) \
        .reshape(18, 128, 4096)
    wexp_h = np.ascontiguousarray(np.concatenate([up_sb, dn_sb], axis=2))

    nc = _get_nc()
    shared = {"wx": wx_h, "wdt": wdt_h, "bdt": bdt_h, "alog": alog_h, "dpar": dpar_h,
              "rw": rw_h, "wexp": wexp_h, "eye": eye_h}
    in_maps = []
    for cidx in range(8):
        m = dict(shared)
        m["u"] = np.ascontiguousarray(u[cidx].T)
        in_maps.append(m)
    res = run_bass_kernel_spmd(nc, in_maps, core_ids=list(range(8)))
    out = np.stack([np.asarray(r["out_fm"]).T for r in res.results], axis=0)
    return np.ascontiguousarray(out.astype(np.float32))


# version 07
# speedup vs baseline: 1.1055x; 1.1055x over previous
import numpy as np
import concourse.bass as bass
import concourse.mybir as mybir
from concourse.bass_utils import run_bass_kernel_spmd

F32 = mybir.dt.float32
BF16 = mybir.dt.bfloat16
AF = mybir.ActivationFunctionType
ALU = mybir.AluOpType
AX = mybir.AxisListType

D = 1024
T = 4096
SC = 512
NSC = T // SC
NDC = 8
NST = 16
NGRP = 4
GN = NST // NGRP
NEXP = 18
NROUT = 16
NET = 4
NEC = 4
NHT = 8
TT2 = 2
LN_EPS = 1e-5

SEM_NAMES = ["dU", "dW", "dWB", "dG", "dS", "dX", "dOut", "dB", "sT", "sV", "sS"]


def build_nc():
    nc = bass.Bass("TRN2", target_bir_lowering=False)

    u_d = nc.dram_tensor("u", [D, T], F32, kind="ExternalInput").ap()
    wx_d = nc.dram_tensor("wx", [128, NDC, 96], F32, kind="ExternalInput").ap()
    wdt_d = nc.dram_tensor("wdt", [64, NDC, 128], F32, kind="ExternalInput").ap()
    bdt_d = nc.dram_tensor("bdt", [128, NDC], F32, kind="ExternalInput").ap()
    alog_d = nc.dram_tensor("alog", [128, NDC * NST], F32, kind="ExternalInput").ap()
    dpar_d = nc.dram_tensor("dpar", [128, NDC], F32, kind="ExternalInput").ap()
    rw_d = nc.dram_tensor("rw", [128, NDC, NST], F32, kind="ExternalInput").ap()
    wexp_d = nc.dram_tensor("wexp", [NEXP, 128, 8192], F32, kind="ExternalInput").ap()
    eye_d = nc.dram_tensor("eye", [128, 128], F32, kind="ExternalInput").ap()

    out_d = nc.dram_tensor("out_fm", [D, T], F32, kind="ExternalOutput").ap()

    x1_scr = nc.dram_tensor("x1_scr", [128, NDC, T], F32, kind="Internal").ap()
    wb_scr = nc.dram_tensor("wb_scr", [NEXP, 128, 8192], BF16, kind="Internal").ap()
    gate_scr = nc.dram_tensor("gate_scr", [NSC, NST, SC], BF16, kind="Internal").ap()
    bc_scr = nc.dram_tensor("bc_scr", [2, 32, SC], BF16, kind="Internal").ap()

    from contextlib import ExitStack
    ctx = ExitStack()
    e = ctx.enter_context

    u_sb = e(nc.sbuf_tensor("u_sb", [128, NDC, SC], F32))
    u_b = e(nc.sbuf_tensor("u_b", [128, 2, SC], BF16))
    dt_sb = e(nc.sbuf_tensor("dt_sb", [128, NDC, SC], F32))
    dtu_sb = e(nc.sbuf_tensor("dtu_sb", [128, SC], BF16))
    bbc_sb = e(nc.sbuf_tensor("bbc_sb", [128, NST, SC], BF16))
    cbc_sb = e(nc.sbuf_tensor("cbc_sb", [128, NST, SC], BF16))
    b_in = e(nc.sbuf_tensor("b_in", [128, GN, SC], BF16))
    h8 = e(nc.sbuf_tensor("h8", [128, GN, SC], BF16))
    abar = e(nc.sbuf_tensor("abar", [128, 4, SC], F32))
    ym_sb = e(nc.sbuf_tensor("ym_sb", [128, SC], F32))
    state_sb = e(nc.sbuf_tensor("state_sb", [128, NDC * NST], BF16))
    dtraw_b = e(nc.sbuf_tensor("dtraw_b", [64, SC], BF16))
    spa_sb = e(nc.sbuf_tensor("spa_sb", [128, SC], F32))
    bcsel_sb = e(nc.sbuf_tensor("bcsel_sb", [32, SC], BF16))
    x1b_sb = e(nc.sbuf_tensor("x1b_sb", [128, 2, NDC, SC], BF16))
    yacc_sb = e(nc.sbuf_tensor("yacc_sb", [128, NHT, SC], F32))
    wslot_sb = e(nc.sbuf_tensor("wslot_sb", [128, 3, 8192], BF16))
    act_sb = e(nc.sbuf_tensor("act_sb", [128, 2, NET, SC], BF16))
    gbc_sb = e(nc.sbuf_tensor("gbc_sb", [128, 2, SC], BF16))
    pb_sb = e(nc.sbuf_tensor("pb_sb", [128, SC], BF16))
    sqb_sb = e(nc.sbuf_tensor("sqb_sb", [128, SC], BF16))
    srow = e(nc.sbuf_tensor("srow", [1, SC], F32))
    sqrow = e(nc.sbuf_tensor("sqrow", [1, SC], F32))
    vrow = e(nc.sbuf_tensor("vrow", [1, SC], F32))
    lg_sb = e(nc.sbuf_tensor("lg_sb", [128, 4, NST], F32))
    wk_sb = e(nc.sbuf_tensor("wk_sb", [128, 4, NST], F32))
    msk_sb = e(nc.sbuf_tensor("msk_sb", [128, 4, NST], F32))
    mx_sb = e(nc.sbuf_tensor("mx_sb", [128, 4], F32))
    ex_sb = e(nc.sbuf_tensor("ex_sb", [128, 4, NST], F32))
    asum_sb = e(nc.sbuf_tensor("asum_sb", [128, 4], F32))
    rs_sb = e(nc.sbuf_tensor("rs_sb", [128, 4], F32))
    gt_sb = e(nc.sbuf_tensor("gt_sb", [128, 4, NST], F32))
    gfm_sb = e(nc.sbuf_tensor("gfm_sb", [16, SC], BF16))
    wx_b = e(nc.sbuf_tensor("wx_b", [128, NDC, 96], BF16))
    wdt_b = e(nc.sbuf_tensor("wdt_b", [64, NDC, 128], BF16))
    bdt_s = e(nc.sbuf_tensor("bdt_s", [128, NDC], F32))
    dp1_s = e(nc.sbuf_tensor("dp1_s", [128, NDC], F32))
    a_s = e(nc.sbuf_tensor("a_s", [128, NDC * NST], F32))
    rw_s = e(nc.sbuf_tensor("rw_s", [128, NDC, NST], F32))
    eye_s = e(nc.sbuf_tensor("eye_s", [128, 128], F32))
    ones_b = e(nc.sbuf_tensor("ones_b", [128, 1], BF16))
    ones1f = e(nc.sbuf_tensor("ones1f", [1, 128], F32))

    ps_big = e(nc.psum_tensor("ps_big", [128, SC], F32))
    ps_up = e(nc.psum_tensor("ps_up", [128, 3, SC], F32))
    ps_y = e(nc.psum_tensor("ps_y", [128, NHT, SC // 2], F32))

    sems = {n: e(nc.semaphore(n)) for n in SEM_NAMES}

    ops = []
    C = {n: 0 for n in SEM_NAMES}

    def emit(eng, fn, waits=(), inc=None):
        ops.append((eng, fn, tuple(w for w in waits if w[1] > 0), inc))
        if inc is not None:
            C[inc[0]] += inc[1]

    def cnt(s):
        return C[s]

    def dma(eng, sem, out, in_, waits=(), accum=False):
        if accum:
            emit(eng, lambda E, o=out, i=in_: E.dma_start(out=o, in_=i, accum_op=ALU.add),
                 waits, (sem, 16))
        else:
            emit(eng, lambda E, o=out, i=in_: E.dma_start(out=o, in_=i), waits, (sem, 16))

    def mm(out, lhsT, rhs, start, stop, waits=(), inc=0):
        emit("tensor",
             lambda E, o=out, l=lhsT, r=rhs, st=start, sp=stop:
             E.matmul(out=o, lhsT=l, rhs=r, start=st, stop=sp),
             waits, (("sT", inc) if inc else None))

    def vop(fn, waits=(), inc=0):
        emit("vector", fn, waits, (("sV", inc) if inc else None))

    def sop(fn, waits=(), inc=0):
        emit("scalar", fn, waits, (("sS", inc) if inc else None))

    def vsync():
        emit("vector", lambda E: None, [("sV", cnt("sV"))], None)

    def ssync():
        emit("scalar", lambda E: None, [("sS", cnt("sS"))], None)

    dma("sync", "dS", bdt_s[:], bdt_d[:])
    dma("sync", "dS", dp1_s[:], dpar_d[:])
    dma("sync", "dS", a_s[:], alog_d[:])
    dma("sync", "dS", rw_s[:], rw_d[:])
    dma("sync", "dS", eye_s[:], eye_d[:])
    dma("gpsimd", "dB", wx_b[:], wx_d[:])
    dma("gpsimd", "dB", wdt_b[:], wdt_d[:])
    init_dS = cnt("dS")
    init_dB = cnt("dB")
    vop(lambda E: E.memset(ones_b[:], 1.0))
    vop(lambda E: E.memset(ones1f[:], 1.0))
    sop(lambda E: E.activation(out=a_s[:], in_=a_s[:], func=AF.Exp),
        waits=[("dS", init_dS)], inc=1)
    vop(lambda E: E.tensor_scalar_mul(out=a_s[:], in0=a_s[:], scalar1=-1.0),
        waits=[("sS", cnt("sS"))], inc=1)
    v_a_neg = cnt("sV")
    vop(lambda E: E.tensor_scalar_add(out=dp1_s[:], in0=dp1_s[:], scalar1=1.0),
        waits=[("dS", init_dS)], inc=1)
    v_dp1 = cnt("sV")

    last = {
        "u_free": [],
        "ub_free": [("sT", 0), ("sT", 0)],
        "psbig_free": ("sS", 0),
        "bc_free": ("sV", 0),
        "x1b_free": [("sT", 0), ("sT", 0)],
        "gbc_free": [("sV", 0)] * 2,
        "wslot_free": [("sT", 0)] * 3,
        "wslot_wb": [("dWB", 0)] * 3,
        "act_free": ("sT", 0),
        "yacc_free": ("dOut", 0),
        "abar_free": [("sV", 0)] * 4,
        "psy_free": ("sV", 0),
        "psup_free": [("sV", 0), ("sV", 0), ("sV", 0)],
        "bcscr_free": [("dB", 0), ("dB", 0)],
        "bcsel_free": ("dS", 0),
        "gt_free": ("sT", 0),
        "gfm_free": ("dS", 0),
    }

    def layer_norm(plane, n_sum_ap, n_sq_ap, bc_ap, ready_waits, sum_war, sq_war, bc_war,
                   sq_start=True):
        t_stats = 0
        for dc in range(NDC):
            w = list(ready_waits(dc))
            if t_stats:
                w.append(("sT", t_stats))
            sop(lambda E, dc=dc: E.activation(out=pb_sb[:], in_=plane[:, dc, :], func=AF.Copy),
                waits=w, inc=1)
            sop(lambda E, dc=dc: E.activation(out=sqb_sb[:], in_=plane[:, dc, :],
                                              func=AF.Square), inc=1)
            s_cast = cnt("sS")
            w0 = [("sS", s_cast)]
            if dc == 0:
                w0 += sum_war
            mm(n_sum_ap, ones_b[:], pb_sb[:], start=(dc == 0), stop=(dc == NDC - 1), waits=w0)
            w1 = []
            if dc == 0:
                w1 += sq_war
            mm(n_sq_ap, ones_b[:], sqb_sb[:], start=(dc == 0 and sq_start),
               stop=(dc == NDC - 1), waits=w1, inc=1)
            t_stats = cnt("sT")
        sop(lambda E: E.activation(out=srow[:], in_=n_sum_ap, func=AF.Copy,
                                   scale=1.0 / D),
            waits=[("sT", t_stats)], inc=1)
        sop(lambda E: E.activation(out=sqrow[:], in_=n_sq_ap, func=AF.Copy,
                                   scale=1.0 / D), inc=1)
        s_sums = cnt("sS")
        vop(lambda E: E.tensor_mul(out=vrow[:], in0=srow[:], in1=srow[:]),
            waits=[("sS", s_sums)], inc=1)
        vsync()
        vop(lambda E: E.tensor_sub(out=vrow[:], in0=sqrow[:], in1=vrow[:]), inc=1)
        vsync()
        vop(lambda E: E.tensor_scalar_add(out=vrow[:], in0=vrow[:], scalar1=LN_EPS), inc=1)
        v_var = cnt("sV")
        sop(lambda E: E.activation(out=vrow[:], in_=vrow[:], func=AF.Ln),
            waits=[("sV", v_var)], inc=1)
        ssync()
        sop(lambda E: E.activation(out=vrow[:], in_=vrow[:], func=AF.Exp, scale=-0.5), inc=1)
        s_inv = cnt("sS")
        mm(bc_ap, ones1f[:], srow[:], start=True, stop=True,
           waits=[("sV", v_var)] + bc_war, inc=1)
        t_mbc = cnt("sT")
        for dc in range(NDC):
            vop(lambda E, dc=dc: E.tensor_sub(out=plane[:, dc, :], in0=plane[:, dc, :],
                                              in1=bc_ap),
                waits=([("sT", t_mbc)] if dc == 0 else ()), inc=(1 if dc == NDC - 1 else 0))
        v_sub = cnt("sV")
        mm(bc_ap, ones1f[:], vrow[:], start=True, stop=True,
           waits=[("sS", s_inv), ("sV", v_sub)], inc=1)
        t_ibc = cnt("sT")
        for dc in range(NDC):
            vop(lambda E, dc=dc: E.tensor_mul(out=plane[:, dc, :], in0=plane[:, dc, :],
                                              in1=bc_ap),
                waits=([("sT", t_ibc)] if dc == 0 else ()), inc=(1 if dc == NDC - 1 else 0))
        return cnt("sV"), s_sums

    GATE_ON_GP = True
    A = [dict() for _ in range(NSC)]

    def issue_uload(c):
        dma("scalar", "dU", u_sb[:],
            u_d.rearrange("(dc p) t -> p dc t", p=128)[:, :, c * SC:(c + 1) * SC],
            waits=last["u_free"])
        A[c]["dU"] = cnt("dU")

    def a_head(c):
        st = A[c]
        dU_c = st["dU"]
        t_xdbl = 0
        for dc in range(NDC):
            w = [("dU", dU_c)]
            w.append(last["ub_free"][dc % 2])
            sop(lambda E, dc=dc: E.activation(out=u_b[:, dc % 2, :], in_=u_sb[:, dc, :],
                                              func=AF.Copy),
                waits=w, inc=1)
            s_cast_dc = cnt("sS")
            w = [("sS", s_cast_dc), ("dB", init_dB)]
            if dc == 0:
                w.append(last["psbig_free"])
            mm(ps_big[0:96, :], wx_b[:, dc, :], u_b[:, dc % 2, :],
               start=(dc == 0), stop=(dc == NDC - 1), waits=w, inc=1)
            t_xdbl = cnt("sT")
            last["ub_free"][dc % 2] = ("sT", t_xdbl)
        st["t_xdbl"] = t_xdbl
        sop(lambda E: E.activation(out=dtraw_b[:], in_=ps_big[0:64, :], func=AF.Copy),
            waits=[("sT", t_xdbl)], inc=1)
        sop(lambda E: E.activation(out=bcsel_sb[:], in_=ps_big[64:96, :], func=AF.Copy),
            waits=[last["bcsel_free"]], inc=1)
        s_bc = cnt("sS")
        st["s_bc"] = s_bc
        last["psbig_free"] = ("sS", s_bc)
        dma("sync", "dS", bc_scr[c % 2, :, :], bcsel_sb[:],
            waits=[("sS", s_bc), last["bcscr_free"][c % 2]])
        dS_bc = cnt("dS")
        last["bcsel_free"] = ("dS", dS_bc)
        dma("sync", "dB", bbc_sb[:],
            bc_scr[c % 2, 0:16, :].partition_broadcast(128),
            waits=[("dS", dS_bc), last["bc_free"]])
        dma("sync", "dB", cbc_sb[:],
            bc_scr[c % 2, 16:32, :].partition_broadcast(128))
        st["dB_bc"] = cnt("dB")
        last["bcscr_free"][c % 2] = ("dB", st["dB_bc"])
        st["v_dt"] = [0] * NDC
        st["s_dtps"] = [0] * NDC
        st["v_prefold"] = [0] * NDC

    filler = []

    def pop_filler():
        if filler:
            filler.pop(0)()

    def flush_filler():
        while filler:
            filler.pop(0)()

    def push_a_dc(c, dc):
        st = A[c]

        def unit_dt():
            v_dt = st["v_dt"]
            s_dtps = st["s_dtps"]
            w = [("sS", st["s_bc"])]
            if dc == 0:
                w.append(last["psbig_free"])
            else:
                w.append(("sS", s_dtps[dc - 1]))
            mm(ps_big[:], wdt_b[:, dc, :], dtraw_b[:], start=True, stop=True, waits=w, inc=1)
            t_here = cnt("sT")
            sop(lambda E, dc=dc: E.activation(out=spa_sb[:], in_=ps_big[:],
                                              func=AF.Abs, bias=bdt_s[:, dc:dc + 1]),
                waits=[("sT", t_here), ("sV", 0 if dc == 0 else v_dt[dc - 1])], inc=1)
            sop(lambda E, dc=dc: E.activation(out=dt_sb[:, dc, :], in_=ps_big[:],
                                              func=AF.Relu, bias=bdt_s[:, dc:dc + 1]), inc=1)
            s_dtps[dc] = cnt("sS")
            sop(lambda E: E.activation(out=spa_sb[:], in_=spa_sb[:], func=AF.Exp,
                                       scale=-1.0), inc=1)
            sop(lambda E: E.activation(out=spa_sb[:], in_=spa_sb[:], func=AF.Ln,
                                       bias=1.0), inc=1)
            s_spl = cnt("sS")
            vop(lambda E, dc=dc: E.tensor_add(out=dt_sb[:, dc, :], in0=dt_sb[:, dc, :],
                                              in1=spa_sb[:]),
                waits=[("sS", s_spl)], inc=1)
            v_dt[dc] = cnt("sV")
            vop(lambda E, dc=dc: E.tensor_mul(out=dtu_sb[:], in0=u_sb[:, dc, :],
                                              in1=dt_sb[:, dc, :]),
                waits=[("dU", st["dU"]), ("sV", v_dp1)], inc=1)

        def unit_grp(grp):
            v_dt = st["v_dt"]
            n0 = grp * GN
            dtu_bc = dtu_sb[:].unsqueeze(1).to_broadcast((128, GN, SC))
            vop(lambda E, dbc=dtu_bc, g=grp: E.tensor_tensor(
                    out=b_in[:], in0=dbc, in1=bbc_sb[:, g * GN:(g + 1) * GN, :],
                    op=ALU.mult),
                waits=[("dB", st["dB_bc"])], inc=1)
            for j in range(GN):
                n = n0 + j
                col = dc * NST + n
                sop(lambda E, dc=dc, col=col: E.activation(
                        out=abar[:, col % 4, :], in_=dt_sb[:, dc, :], func=AF.Exp,
                        scale=a_s[:, col:col + 1]),
                    waits=[("sV", v_dt[dc]),
                           last["abar_free"][col % 4]], inc=1)
                s_exp = cnt("sS")
                init = 0.0 if c == 0 else state_sb[:, col:col + 1]
                vop(lambda E, j=j, col=col, init=init: E.tensor_tensor_scan(
                        out=h8[:, j, :], data0=abar[:, col % 4, :], data1=b_in[:, j, :],
                        initial=init, op0=ALU.mult, op1=ALU.add),
                    waits=[("sS", s_exp)], inc=1)
                last["abar_free"][col % 4] = ("sV", cnt("sV"))
            vsync()
            vop(lambda E, dc=dc, n0=n0: E.tensor_copy(
                    out=state_sb[:, dc * NST + n0: dc * NST + n0 + GN],
                    in_=h8[:, :, SC - 1]), inc=1)
            vop(lambda E, g=grp: E.tensor_tensor(
                    out=h8[:].rearrange("p a b -> p (a b)"),
                    in0=h8[:].rearrange("p a b -> p (a b)"),
                    in1=cbc_sb[:, g * GN:(g + 1) * GN, :].rearrange("p a b -> p (a b)"),
                    op=ALU.mult), inc=1)
            vop(lambda E: E.tensor_add(out=h8[:, 0:2, :].rearrange("p a b -> p (a b)"),
                                       in0=h8[:, 0:2, :].rearrange("p a b -> p (a b)"),
                                       in1=h8[:, 2:4, :].rearrange("p a b -> p (a b)")))
            if grp == 0:
                vop(lambda E: E.tensor_add(out=ym_sb[:], in0=h8[:, 0, :], in1=h8[:, 1, :]),
                    inc=1)
            else:
                vop(lambda E: E.tensor_add(out=h8[:, 0, :], in0=h8[:, 0, :], in1=h8[:, 1, :]))
                vop(lambda E: E.tensor_add(out=ym_sb[:], in0=ym_sb[:], in1=h8[:, 0, :]),
                    inc=1)
            if grp == NGRP - 1:
                vop(lambda E, dc=dc: E.scalar_tensor_tensor(
                        out=u_sb[:, dc, :], in0=u_sb[:, dc, :], scalar=dp1_s[:, dc:dc + 1],
                        in1=ym_sb[:], op0=ALU.mult, op1=ALU.add), inc=1)
                st["v_prefold"][dc] = cnt("sV")
                if dc == NDC - 1:
                    last["bc_free"] = ("sV", st["v_prefold"][NDC - 1])

        filler.append(unit_dt)
        for grp in range(NGRP):
            filler.append(lambda g=grp: unit_grp(g))

    def a_tail(c):
        st = A[c]
        v_prefold = st["v_prefold"]
        v_x1, s_sums1 = layer_norm(
            u_sb, ps_big[0:1, :], ps_big[32:33, :], ps_big[:],
            ready_waits=lambda dc: [("sV", v_prefold[dc])],
            sum_war=[("sS", st["s_dtps"][NDC - 1])],
            sq_war=[],
            bc_war=[],
            sq_start=False,
        )
        sop(lambda E, c=c: E.activation(
                out=x1b_sb[:, c % 2, :, :].rearrange("p a b -> p (a b)"),
                in_=u_sb[:].rearrange("p a b -> p (a b)"), func=AF.Copy),
            waits=[("sV", v_x1), last["x1b_free"][c % 2]], inc=1)
        st["s_x1b"] = cnt("sS")
        dma("sync", "dS", x1_scr[:, :, c * SC:(c + 1) * SC], u_sb[:], waits=[("sV", v_x1)])
        st["dS_x1"] = cnt("dS")

        for tt in range(4):
            for dc in range(NDC):
                w = [("sV", v_x1)]
                mm(ps_big[:, tt * NST:(tt + 1) * NST],
                   u_sb[:, dc, tt * 128:(tt + 1) * 128], rw_s[:, dc, :],
                   start=(tt == 0 and dc == 0), stop=(tt == 3 and dc == NDC - 1),
                   waits=w, inc=(1 if tt == 3 and dc == NDC - 1 else 0))
        t_lg = cnt("sT")
        sop(lambda E: E.activation(out=lg_sb[:].rearrange("p a b -> p (a b)"),
                                   in_=ps_big[:, 0:64], func=AF.Copy),
            waits=[("sT", t_lg)], inc=1)
        s_lg = cnt("sS")
        vop(lambda E: E.tensor_copy(out=wk_sb[:].rearrange("p a b -> p (a b)"),
                                    in_=lg_sb[:].rearrange("p a b -> p (a b)")),
            waits=[("sS", s_lg), last["gt_free"]], inc=1)
        for it in range(4):
            vsync()
            vop(lambda E: E.tensor_reduce(out=mx_sb[:], in_=wk_sb[:], axis=AX.X, op=ALU.max),
                inc=1)
            if it < 3:
                vsync()
                mx_bc = mx_sb[:].unsqueeze(2).to_broadcast((128, 4, NST))
                vop(lambda E, m=mx_bc: E.tensor_tensor(out=msk_sb[:], in0=wk_sb[:], in1=m,
                                                       op=ALU.is_ge), inc=1)
                vsync()
                vop(lambda E: E.scalar_tensor_tensor(
                        out=wk_sb[:].rearrange("p a b -> p (a b)"),
                        in0=msk_sb[:].rearrange("p a b -> p (a b)"), scalar=-1e30,
                        in1=wk_sb[:].rearrange("p a b -> p (a b)"),
                        op0=ALU.mult, op1=ALU.add), inc=1)
        vsync()
        mx_bc = mx_sb[:].unsqueeze(2).to_broadcast((128, 4, NST))
        vop(lambda E, m=mx_bc: E.tensor_tensor(out=msk_sb[:], in0=lg_sb[:], in1=m,
                                               op=ALU.is_ge), inc=1)
        vop(lambda E: E.tensor_reduce(out=mx_sb[:], in_=lg_sb[:], axis=AX.X, op=ALU.max,
                                      negate=True), inc=1)
        v_topk = cnt("sV")
        for tt in range(4):
            sop(lambda E, tt=tt: E.activation(out=ex_sb[:, tt, :], in_=lg_sb[:, tt, :],
                                              func=AF.Exp, bias=mx_sb[:, tt:tt + 1],
                                              accum_out=asum_sb[:, tt:tt + 1]),
                waits=([("sV", v_topk)] if tt == 0 else ()), inc=(1 if tt == 3 else 0))
        s_ex = cnt("sS")
        vop(lambda E: E.reciprocal(out=rs_sb[:], in_=asum_sb[:]),
            waits=[("sS", s_ex)], inc=1)
        vsync()
        rs_bc = rs_sb[:].unsqueeze(2).to_broadcast((128, 4, NST))
        vop(lambda E, rb=rs_bc: E.tensor_tensor(out=gt_sb[:], in0=ex_sb[:], in1=rb,
                                                op=ALU.mult), inc=1)
        vsync()
        vop(lambda E: E.tensor_mul(out=gt_sb[:].rearrange("p a b -> p (a b)"),
                                   in0=gt_sb[:].rearrange("p a b -> p (a b)"),
                                   in1=msk_sb[:].rearrange("p a b -> p (a b)")), inc=1)
        v_gate = cnt("sV")
        s_gfm = cnt("sS")
        for tt in range(4):
            w = [("sV", v_gate), ("dS", init_dS)]
            if tt == 0:
                w.append(("sS", s_lg))
            else:
                w.append(("sS", s_gfm))
            emit("tensor",
                 lambda E, tt=tt: E.transpose(out=ps_big[0:16, tt * 128:(tt + 1) * 128],
                                              in_=gt_sb[:, tt, :], identity=eye_s[:]),
                 w, ("sT", 1))
            t_tr = cnt("sT")
            sop(lambda E, tt=tt: E.activation(out=gfm_sb[:, tt * 128:(tt + 1) * 128],
                                              in_=ps_big[0:16, tt * 128:(tt + 1) * 128],
                                              func=AF.Copy),
                waits=[("sT", t_tr)] + ([last["gfm_free"]] if tt == 0 else []), inc=1)
            s_gfm = cnt("sS")
        last["psbig_free"] = ("sS", s_gfm)
        last["gt_free"] = ("sT", cnt("sT"))
        dma("sync", "dS", gate_scr[c, :, :], gfm_sb[:], waits=[("sS", s_gfm)])
        st["dS_gate"] = cnt("dS")
        last["gfm_free"] = ("dS", st["dS_gate"])
        last["u_free"] = [("sV", v_x1), ("sS", st["s_x1b"]), ("dS", st["dS_x1"]),
                          ("sT", t_lg)]

    def load_expert(c, st, r):
        slot = r % 3
        w = [last["wslot_free"][slot]]
        if c == 0:
            w.append(last["wslot_wb"][slot])
            dma("gpsimd", "dW", wslot_sb[:, slot, :], wexp_d[r], waits=w)
            st["dW_after"][r] = cnt("dW")
            dma("scalar", "dWB", wb_scr[r], wslot_sb[:, slot, :],
                waits=[("dW", st["dW_after"][r])])
            last["wslot_wb"][slot] = ("dWB", cnt("dWB"))
        else:
            dma("scalar", "dW", wslot_sb[:, slot, :], wb_scr[r],
                waits=w + [("dWB", last["wb_done"])])
            st["dW_after"][r] = cnt("dW")

    def load_gate(c, st, r):
        dma("sync", "dG", gbc_sb[:, r % 2, :],
            gate_scr[c, r, :].partition_broadcast(128),
            waits=[("dS", A[c]["dS_gate"]), last["gbc_free"][r % 2]])
        st["dG_after"][r] = cnt("dG")

    B = [dict() for _ in range(NSC)]

    def b_head(c):
        st = B[c]
        st["dW_after"] = {}
        st["dG_after"] = {}
        st["upslot"] = 0
        for r in range(3):
            load_expert(c, st, r)
        for r in range(2):
            load_gate(c, st, r)
        if c + 1 < NSC:
            issue_uload(c + 1)

    def b_group(c, g):
        st = B[c]
        rr = [2 * g, 2 * g + 1] if g < 8 else [16, 17]
        s_gelu = {}
        v_gmul = {}
        g_eng = "gpsimd" if GATE_ON_GP else "vector"
        g_sem = "dB" if GATE_ON_GP else "sV"

        def ups_for(ri, r):
            for et in range(NET):
                for dc in range(NDC):
                    w = []
                    if dc == 0:
                        w = [("dW", st["dW_after"][r]), ("sS", A[c]["s_x1b"]),
                             last["psup_free"][st["upslot"]]]
                    mm(ps_up[:, st["upslot"], :],
                       wslot_sb[:, r % 3, dc * 512 + et * 128: dc * 512 + (et + 1) * 128],
                       x1b_sb[:, c % 2, dc, :],
                       start=(dc == 0), stop=(dc == NDC - 1), waits=w,
                       inc=(1 if dc == NDC - 1 else 0))
                t_up = cnt("sT")
                sop(lambda E, ri=ri, et=et, sl=st["upslot"]: E.activation(
                        out=act_sb[:, ri, et, :], in_=ps_up[:, sl, :], func=AF.Gelu),
                    waits=[("sT", t_up), last["act_free"]], inc=1)
                s_gelu[(ri, et)] = cnt("sS")
                last["psup_free"][st["upslot"]] = ("sS", cnt("sS"))
                st["upslot"] = (st["upslot"] + 1) % 3
                if r < NROUT:
                    emit(g_eng,
                         lambda E, ri=ri, ec=et, r=r: E.tensor_tensor(
                             out=act_sb[:, ri, ec, :], in0=act_sb[:, ri, ec, :],
                             in1=gbc_sb[:, r % 2, :],
                             op=ALU.mult),
                         [("sS", s_gelu[(ri, et)]), ("dG", st["dG_after"][r])],
                         (g_sem, 1))
                    v_gmul[(ri, et)] = (g_sem, cnt(g_sem))
            pop_filler()
            if r < NROUT:
                last["gbc_free"][r % 2] = (g_sem, cnt(g_sem))
                if r + 2 < NROUT:
                    load_gate(c, st, r + 2)
            else:
                for et in range(NET):
                    v_gmul[(ri, et)] = None

        def downs_for(ri, r, tt2, start_sess, stop_sess):
            for ec in range(NEC):
                for ht in range(NHT):
                    w = []
                    if ht == 0:
                        if v_gmul[(ri, ec)] is not None:
                            w.append(v_gmul[(ri, ec)])
                        else:
                            w.append(("sS", s_gelu[(ri, ec)]))
                        if start_sess and ec == 0:
                            w.append(last["psy_free"])
                    mm(ps_y[:, ht, :],
                       wslot_sb[:, r % 3,
                                4096 + ec * 1024 + ht * 128:4096 + ec * 1024 + (ht + 1) * 128],
                       act_sb[:, ri, ec, tt2 * 256:(tt2 + 1) * 256],
                       start=(start_sess and ec == 0 and ht % 2 == 0),
                       stop=(stop_sess and ec == NEC - 1),
                       waits=w,
                       inc=(1 if stop_sess and ec == NEC - 1 and ht == NHT - 1 else 0))

        def evacs(tt2):
            t_down = cnt("sT")
            for hh in range(2):
                psv = ps_y[:, hh * 4:(hh + 1) * 4, :]
                ysl = yacc_sb[:, hh * 4:(hh + 1) * 4, tt2 * 256:(tt2 + 1) * 256]
                w = [("sT", t_down)]
                if g == 0 and hh == 0:
                    w.append(last["yacc_free"])
                if g == 0:
                    vop(lambda E, pv=psv, ys=ysl: E.tensor_copy(out=ys, in_=pv),
                        waits=w, inc=1)
                else:
                    vop(lambda E, pv=psv, ys=ysl: E.tensor_add(out=ys, in0=ys, in1=pv),
                        waits=w, inc=1)
                st["v_evac"][(tt2, hh)] = cnt("sV")
            last["psy_free"] = ("sV", cnt("sV"))

        st.setdefault("v_evac", {})
        ups_for(0, rr[0])
        downs_for(0, rr[0], 0, start_sess=True, stop_sess=False)
        ups_for(1, rr[1])
        downs_for(1, rr[1], 0, start_sess=False, stop_sess=True)
        evacs(0)
        pop_filler()
        downs_for(0, rr[0], 1, start_sess=True, stop_sess=False)
        downs_for(1, rr[1], 1, start_sess=False, stop_sess=True)
        evacs(1)
        pop_filler()
        t_g_end = cnt("sT")
        last["act_free"] = ("sT", t_g_end)
        for r in rr:
            last["wslot_free"][r % 3] = ("sT", t_g_end)
            if r + 3 < NEXP:
                load_expert(c, st, r + 3)

    def b_tail(c):
        st = B[c]
        if c == 0:
            last["wb_done"] = cnt("dWB")
        last["x1b_free"][c % 2] = ("sT", cnt("sT"))
        dX_hh = [0, 0]
        for hh in range(2):
            dma("gpsimd", "dX", yacc_sb[:, hh * 4:(hh + 1) * 4, :],
                x1_scr[:, hh * 4:(hh + 1) * 4, c * SC:(c + 1) * SC],
                waits=[("sV", st["v_evac"][(1, hh)]), ("dS", A[c]["dS_x1"])], accum=True)
            dX_hh[hh] = cnt("dX")
        v_x2, _ = layer_norm(
            yacc_sb, ps_up[0:1, 0, :], ps_up[32:33, 0, :], ps_up[:, 1, :],
            ready_waits=lambda dc: [("dX", dX_hh[dc // 4])],
            sum_war=[last["psup_free"][0]],
            sq_war=[],
            bc_war=[last["psup_free"][1]],
            sq_start=False,
        )
        last["psup_free"] = [("sV", v_x2), ("sV", v_x2), ("sV", v_x2)]
        dma("sync", "dOut",
            out_d.rearrange("(dc p) t -> p dc t", p=128)[:, :, c * SC:(c + 1) * SC],
            yacc_sb[:], waits=[("sV", v_x2)])
        last["yacc_free"] = ("dOut", cnt("dOut"))

    issue_uload(0)
    a_head(0)
    for dc in range(NDC):
        push_a_dc(0, dc)
    flush_filler()
    a_tail(0)
    for c in range(NSC):
        b_head(c)
        b_group(c, 0)
        if c + 1 < NSC:
            a_head(c + 1)
            for dc in range(NDC):
                push_a_dc(c + 1, dc)
        for g in range(1, 9):
            b_group(c, g)
        flush_filler()
        b_tail(c)
        if c + 1 < NSC:
            a_tail(c + 1)

    emit("sync", lambda E: None, [("dOut", cnt("dOut"))], None)

    with nc.Block() as block:
        def replay(name):
            def f(E):
                for eng, fn, waits, inc in ops:
                    if eng != name:
                        continue
                    for s, v in waits:
                        E.wait_ge(sems[s], v)
                    ins = fn(E)
                    if inc is not None and ins is not None:
                        ins.then_inc(sems[inc[0]], inc[1])
            return f

        block.sync(replay("sync"))
        block.gpsimd(replay("gpsimd"))
        block.tensor(replay("tensor"))
        block.vector(replay("vector"))
        block.scalar(replay("scalar"))

    ctx.close()
    return nc


_NC_CACHE = None


def _get_nc():
    global _NC_CACHE
    if _NC_CACHE is None:
        _NC_CACHE = build_nc()
    return _NC_CACHE


def kernel(L0_attn_u, L0_attn_A_log, L0_attn_W_x, L0_attn_W_dt, L0_attn_b_dt,
           L0_attn_D_param, L0_ffn_shared_up, L0_ffn_shared_down, L0_ffn_routed_up,
           L0_ffn_routed_down, L0_ffn_router_w):
    u = np.asarray(L0_attn_u, dtype=np.float32)
    W_x = np.asarray(L0_attn_W_x, dtype=np.float32)
    W_dt = np.asarray(L0_attn_W_dt, dtype=np.float32)
    b_dt = np.asarray(L0_attn_b_dt, dtype=np.float32)
    A_log = np.asarray(L0_attn_A_log, dtype=np.float32)
    D_par = np.asarray(L0_attn_D_param, dtype=np.float32)
    sh_up = np.asarray(L0_ffn_shared_up, dtype=np.float32)
    sh_dn = np.asarray(L0_ffn_shared_down, dtype=np.float32)
    ro_up = np.asarray(L0_ffn_routed_up, dtype=np.float32)
    ro_dn = np.asarray(L0_ffn_routed_down, dtype=np.float32)
    r_w = np.asarray(L0_ffn_router_w, dtype=np.float32)

    wx_h = np.ascontiguousarray(W_x.T.reshape(8, 128, 96).transpose(1, 0, 2))
    wdt_h = np.ascontiguousarray(W_dt.reshape(8, 128, 64).transpose(2, 0, 1))
    bdt_h = np.ascontiguousarray(b_dt.reshape(8, 128).T)
    alog_h = np.ascontiguousarray(
        A_log.reshape(8, 128, 16).transpose(1, 0, 2).reshape(128, 128))
    dpar_h = np.ascontiguousarray(D_par.reshape(8, 128).T)
    rw_h = np.ascontiguousarray(r_w.T.reshape(8, 128, 16).transpose(1, 0, 2))
    eye_h = np.eye(128, dtype=np.float32)

    ups = np.concatenate([ro_up, sh_up], axis=0)
    dns = np.concatenate([ro_dn, sh_dn], axis=0)
    up_sb = ups.transpose(0, 2, 1).reshape(18, 8, 128, 512).transpose(0, 2, 1, 3) \
        .reshape(18, 128, 4096)
    dn_sb = dns.transpose(0, 2, 1).reshape(18, 4, 128, 1024).transpose(0, 2, 1, 3) \
        .reshape(18, 128, 4096)
    wexp_h = np.ascontiguousarray(np.concatenate([up_sb, dn_sb], axis=2))

    nc = _get_nc()
    shared = {"wx": wx_h, "wdt": wdt_h, "bdt": bdt_h, "alog": alog_h, "dpar": dpar_h,
              "rw": rw_h, "wexp": wexp_h, "eye": eye_h}
    in_maps = []
    for cidx in range(8):
        m = dict(shared)
        m["u"] = np.ascontiguousarray(u[cidx].T)
        in_maps.append(m)
    res = run_bass_kernel_spmd(nc, in_maps, core_ids=list(range(8)))
    out = np.stack([np.asarray(r["out_fm"]).T for r in res.results], axis=0)
    return np.ascontiguousarray(out.astype(np.float32))
